# version 02 (final)
# baseline (speedup 1.0000x reference)
import numpy as np
import ml_dtypes
import concourse.bass as bass
import concourse.mybir as mybir
from concourse.bass_utils import run_bass_kernel_spmd
from contextlib import ExitStack

F32 = mybir.dt.float32
BF16 = mybir.dt.bfloat16
AF = mybir.ActivationFunctionType
ALU = mybir.AluOpType
AX = mybir.AxisListType

B, S, D = 4, 4096, 1024
TOK = 2048
NT = 4
NC = 8
H, DH, HP = 16, 64, 8
DQL = 512
MAGIC = 12582912.0

KPART = 4 * 16 * 128 * 256
VPART = 16 * 128 * 1040
MPART = 128 * 16
KVLEN = KPART + VPART + MPART

N_STAGES = 4
DBG = False


class Prog:
    def __init__(self, nc):
        self.nc = nc
        self.by_eng = {"sync": [], "vector": [], "scalar": [], "tensor": [], "gpsimd": []}
        self.counts = {}
        self.names = {}
        self.chain = {}
        self.last_done = {}

    def sem(self, ctx, name):
        s = ctx.enter_context(self.nc.semaphore(name))
        self.counts[name] = 0
        self.names[id(s)] = name
        return s

    def val(self, sem):
        return self.counts[self.names[id(sem)]]

    def op(self, eng, fn, sem=None, n=1):
        chain = self.chain.get(eng)
        if chain is None:
            if sem is not None:
                self.counts[self.names[id(sem)]] += n
                self.by_eng[eng].append(lambda e, fn=fn, sem=sem, n=n: fn(e).then_inc(sem, n))
                return self.counts[self.names[id(sem)]]
            self.by_eng[eng].append(fn)
            return None
        prev = self.last_done.get(eng)
        use = sem if sem is not None else chain
        self.counts[self.names[id(use)]] += (n if sem is not None else 1)
        val = self.counts[self.names[id(use)]]

        def go(e, fn=fn, prev=prev, use=use, val_n=(n if sem is not None else 1)):
            inst = fn(e)
            if prev is not None:
                inst._wait_ge(prev[0], prev[1])
            inst.then_inc(use, val_n)
        self.by_eng[eng].append(go)
        self.last_done[eng] = (use, val)
        return val

    def dma(self, eng, sem, out, in_, fresh=True, **kw):
        nm = self.names[id(sem)]
        if fresh and self.counts[nm] > 0:
            self.wait(eng, sem)
        self.counts[nm] += 16
        self.by_eng[eng].append(
            lambda e, out=out, in_=in_, kw=kw, sem=sem: e.dma_start(out=out, in_=in_, **kw).then_inc(sem, 16))
        return self.counts[nm]

    def wait(self, eng, sem, v=None):
        tgt = self.counts[self.names[id(sem)]] if v is None else v
        if tgt > 0:
            self.by_eng[eng].append(lambda e, sem=sem, tgt=tgt: e.wait_ge(sem, tgt))
        return tgt

    def run(self, block):
        def runner(fns):
            def go(e):
                for f in fns:
                    f(e)
            return go
        block.sync(runner(self.by_eng["sync"]))
        block.vector(runner(self.by_eng["vector"]))
        block.scalar(runner(self.by_eng["scalar"]))
        block.tensor(runner(self.by_eng["tensor"]))
        block.gpsimd(runner(self.by_eng["gpsimd"]))


def _pp(t, col):
    return t[:, col:col + 1]


def build_nc(n_stages=N_STAGES, world=8):
    nc = bass.Bass("TRN2", target_bir_lowering=False, debug=False)
    di = {}

    def inp(name, shape, dtype=F32):
        di[name] = nc.dram_tensor(name, list(shape), dtype, kind="ExternalInput")
        return di[name]

    inp("x_in", [D, TOK])
    inp("l0_wdq", [D, DQL]); inp("l0_wdkv", [D, DQL])
    inp("l0_wuq", [DQL, D]); inp("l0_wuk", [DQL, D]); inp("l0_wuv", [DQL, D])
    inp("l0_wo", [D, D])
    inp("l0_lnp", [128, 2 * NC])
    inp("l0_w1t", [32, NC, 128, 128]); inp("l0_w2t", [32, NC, 128, 128])
    inp("l0_h1t", [32, 32, 128, 128]); inp("l0_h2t", [32, 32, 128, 128])
    inp("l0_w3t", [NC, 32, 128, 128])
    inp("l0_fs", [128, 32 * 4 + NC])
    inp("l1_wqt", [HP, NC, 128, 128]); inp("l1_wkt", [HP, NC, 128, 128])
    inp("l1_wv", [D, D])
    inp("l1_wo", [H, DH, D])
    inp("l1_b", [128, 2 * NC])
    inp("l1_bv_row", [1, D]); inp("l1_bo_row", [1, D])
    inp("l1_lnp", [128, 2 * NC])
    inp("l1_masks", [NT, 8, 128, 512], BF16)
    inp("l1_wup_oi", [3072, D]); inp("l1_wgate_oi", [3072, D]); inp("l1_wdown_oi", [D, 3072])
    inp("l1_wup_io", [D, 3072]); inp("l1_wgate_io", [D, 3072]); inp("l1_wdown_io", [3072, D])
    inp("ropeC", [128, TOK], BF16); inp("ropeS", [128, TOK], BF16)
    inp("ones_f", [128, 128])
    inp("ident", [128, 128])
    inp("ones512", [1, 512])
    inp("sel2", [2, 256])
    inp("rep4", [4, 128])
    inp("permT", [128, 128], BF16)
    inp("cst", [128, 8])

    out_d = nc.dram_tensor("out_x", [D, TOK], F32, kind="ExternalOutput")
    dbg_t = {}
    if DBG:
        for nm, shape, dt in (("dbg_h", [128, NC * TOK], BF16),
                              ("dbg_cq", [128, 4 * TOK], BF16),
                              ("dbg_ckv", [128, 4 * TOK], BF16),
                              ("dbg_q", [128, HP * TOK], BF16),
                              ("dbg_kv", [KVLEN], BF16),
                              ("dbg_mk", [128, 16], F32),
                              ("dbg_bias", [128, 64], F32),
                              ("dbg_oqg", [64, H * 512], BF16),
                              ("dbg_oqg1", [64, H * 512], BF16),
                              ("dbg_oqg2", [64, H * 512], BF16),
                              ("dbg_oqg3", [64, H * 512], BF16)):
            dbg_t[nm] = nc.dram_tensor(nm, shape, dt, kind="ExternalOutput")

    kv_in = nc.dram_tensor("kv_in", [KVLEN], BF16, kind="Internal")
    kv_out = nc.dram_tensor("kv_out", [2, KVLEN], BF16, kind="Internal")
    mx_io = [(nc.dram_tensor(f"mx{k}_in", [1, 2], F32, kind="Internal"),
              nc.dram_tensor(f"mx{k}_out", [1, 2], F32, kind="Internal")) for k in range(4)]
    up_raw_d = nc.dram_tensor("up_raw_d", [24, 128, TOK], F32, kind="Internal")
    gate_raw_d = nc.dram_tensor("gate_raw_d", [24, 128, TOK], F32, kind="Internal")
    t2_d = nc.dram_tensor("t2_d", [24, 128, TOK], F32, kind="Internal")
    t3_d = nc.dram_tensor("t3_d", [24, 128, TOK], F32, kind="Internal")
    wq_up_d = nc.dram_tensor("wq_up_d", [NC, 128, 3072], BF16, kind="Internal")
    wq_gate_d = nc.dram_tensor("wq_gate_d", [NC, 128, 3072], BF16, kind="Internal")
    wq_down_d = nc.dram_tensor("wq_down_d", [24, 128, D], BF16, kind="Internal")
    sT_d = nc.dram_tensor("sT_d", [96, 3072], F32, kind="Internal")
    rsT_d = nc.dram_tensor("rsT_d", [96, 3072], F32, kind="Internal")
    y0_d = nc.dram_tensor("y0_d", [NT, 128, NC * 512], F32, kind="Internal")

    def kv_k(buf, slot, g4, tile):
        flat = buf.ap() if slot is None else buf.ap()[slot]
        off = (g4 * 16 + tile) * (128 * 256)
        return flat[off:off + 128 * 256].rearrange("(p n) -> p n", p=128)

    def kv_v(buf, slot, tile):
        flat = buf.ap() if slot is None else buf.ap()[slot]
        off = KPART + tile * (128 * 1040)
        return flat[off:off + 128 * 1040].rearrange("(p n) -> p n", p=128)

    def kv_m(buf, slot):
        flat = buf.ap() if slot is None else buf.ap()[slot]
        off = KPART + VPART
        return flat[off:off + MPART].rearrange("(p n) -> p n", p=128).bitcast(F32)

    ctx = ExitStack()
    with ctx:
        E = ctx.enter_context
        P = Prog(nc)

        x_fm = E(nc.sbuf_tensor("x_fm", [128, NC * TOK], F32))
        h_bf = E(nc.sbuf_tensor("h_bf", [128, NC * TOK], BF16))
        ones_f = E(nc.sbuf_tensor("ones_f_sb", [128, 128], F32))
        ident = E(nc.sbuf_tensor("ident_sb", [128, 128], F32))
        ones512 = E(nc.sbuf_tensor("ones512_sb", [1, 512], F32))
        sel2 = E(nc.sbuf_tensor("sel2_sb", [2, 256], F32))
        rep4 = E(nc.sbuf_tensor("rep4_sb", [4, 128], F32))
        permT = E(nc.sbuf_tensor("permT_sb", [128, 128], BF16))
        cst = E(nc.sbuf_tensor("cst_sb", [128, 8], F32))
        lnp = E(nc.sbuf_tensor("lnp_sb", [128, 2 * NC], F32))
        qkb = E(nc.sbuf_tensor("qkb_sb", [128, 2 * NC], F32))
        scl = E(nc.sbuf_tensor("scl_sb", [128, 12], F32))
        mrow = E(nc.sbuf_tensor("mrow_sb", [1, 128], F32))
        mxs = E(nc.sbuf_tensor("mxs_sb", [1, 16], F32))
        lnrow = E(nc.sbuf_tensor("lnrow_sb", [1, 512], F32))

        s_in = P.sem(ctx, "s_in")
        s_hl = P.sem(ctx, "s_hl")
        s_wl = P.sem(ctx, "s_wl")
        s_ms = P.sem(ctx, "s_ms")
        s_mm = P.sem(ctx, "s_mm")
        s_ve = P.sem(ctx, "s_ve")
        s_sc = P.sem(ctx, "s_sc")
        s_ex = P.sem(ctx, "s_ex")
        s_cc = P.sem(ctx, "s_cc")
        s_st = P.sem(ctx, "s_st")
        s_kv = [P.sem(ctx, "s_kv0"), P.sem(ctx, "s_kv1")]
        s_w = [P.sem(ctx, "s_w0"), P.sem(ctx, "s_w1")]
        s_e = [P.sem(ctx, "s_e0"), P.sem(ctx, "s_e1"), P.sem(ctx, "s_e2"), P.sem(ctx, "s_e3")]
        s_l = [P.sem(ctx, "s_l0"), P.sem(ctx, "s_l1")]
        s_q = [P.sem(ctx, "s_q0"), P.sem(ctx, "s_q1")]
        P.chain = {"vector": s_ve, "scalar": s_sc}

        def phase_barrier():
            for eng in ("gpsimd", "sync"):
                P.wait(eng, s_mm)
                P.wait(eng, s_ve)
                P.wait(eng, s_sc)
                P.wait(eng, s_ex)
                P.wait(eng, s_e[0])
                P.wait(eng, s_e[1])
                P.wait(eng, s_e[2])
                P.wait(eng, s_e[3])
                P.wait(eng, s_wl)
                P.wait(eng, s_w[0])
                P.wait(eng, s_w[1])
                P.wait(eng, s_l[0])
                P.wait(eng, s_l[1])
                P.wait(eng, s_kv[0])
                P.wait(eng, s_kv[1])
                P.wait(eng, s_q[0])
                P.wait(eng, s_q[1])

        def dump(nm, src):
            if not DBG or nm not in dbg_t:
                return
            P.wait("sync", s_ve)
            P.wait("sync", s_sc)
            P.wait("sync", s_mm)
            P.wait("sync", s_ex)
            P.dma("sync", s_st, dbg_t[nm].ap(), src)

        for c in range(NC):
            P.dma("sync", s_in, x_fm[:, c * TOK:(c + 1) * TOK],
                  di["x_in"].ap()[c * 128:(c + 1) * 128, :], fresh=(c == 0))
        for nm, t in (("ones_f", ones_f), ("ident", ident), ("ones512", ones512),
                      ("sel2", sel2), ("rep4", rep4), ("permT", permT), ("cst", cst)):
            P.dma("sync", s_in, t[:, :], di[nm].ap(), fresh=False)
        IN_ALL = P.val(s_in)
        for eng in ("vector", "scalar", "tensor", "gpsimd"):
            P.wait(eng, s_in, IN_ALL)

        def fused_ln(affine, out_t, occ, psA, psB, scr, yspill=None):
            for tt in range(NT):
                mm_marks = {}
                for c in range(NC):
                    sl = slice(0, 512) if c % 2 == 0 else slice(1024, 1536)
                    xc = x_fm[:, c * TOK + tt * 512: c * TOK + (tt + 1) * 512]
                    if c >= 2:
                        P.wait("vector", s_mm, mm_marks[c - 2])
                    P.op("vector", lambda e, xc=xc, sl=sl: e.tensor_mul(
                        out=scr[:, sl], in0=xc, in1=xc), s_ve)
                    P.wait("tensor", s_ve)
                    P.op("tensor", lambda e, xc=xc, c=c: e.matmul(
                        psA[0:1, :], ones_f[:, 0:1], xc, start=(c == 0), stop=(c == NC - 1)))
                    mm_marks[c] = P.op("tensor", lambda e, c=c, sl=sl: e.matmul(
                        psB[0:1, :], ones_f[:, 0:1], scr[:, sl], start=(c == 0), stop=(c == NC - 1)),
                        s_mm)
                P.wait("vector", s_mm)
                P.op("vector", lambda e: e.tensor_scalar(
                    out=scr[0:1, 0:512], in0=psA[0:1, :], scalar1=1.0 / 1024.0,
                    scalar2=None, op0=ALU.mult))
                P.op("vector", lambda e: e.tensor_mul(
                    out=scr[0:1, 512:1024], in0=scr[0:1, 0:512], in1=scr[0:1, 0:512]))
                P.op("vector", lambda e: e.scalar_tensor_tensor(
                    out=scr[0:1, 512:1024], in0=psB[0:1, :], scalar=1.0 / 1024.0,
                    in1=scr[0:1, 512:1024], op0=ALU.mult, op1=ALU.subtract), s_ve)
                P.wait("scalar", s_ve)
                P.op("scalar", lambda e: e.activation(
                    out=scr[0:1, 1024:1536], in_=scr[0:1, 512:1024], func=AF.Sqrt,
                    bias=cst[0:1, 0:1], scale=1.0), s_sc)
                P.wait("vector", s_sc)
                P.op("vector", lambda e: e.reciprocal(
                    out=scr[0:1, 1536:2048], in_=scr[0:1, 1024:1536]))
                if yspill is not None:
                    psC, ystg, yex = yspill
                    P.wait("tensor", s_ve)
                    P.op("tensor", lambda e: e.matmul(
                        psC[:, :], ones_f[0:1, :], scr[0:1, 1536:2048], start=True, stop=True), s_mm)
                    P.wait("vector", s_mm)
                if affine:
                    P.op("vector", lambda e: e.tensor_mul(
                        out=scr[0:1, 512:1024], in0=scr[0:1, 512:1024], in1=scr[0:1, 1536:2048]))
                    P.op("vector", lambda e: e.tensor_mul(
                        out=scr[0:1, 512:1024], in0=scr[0:1, 512:1024], in1=scr[0:1, 1536:2048]), s_ve)
                    P.wait("scalar", s_ve)
                    P.op("scalar", lambda e: e.activation(
                        out=scr[0:1, 1024:1536], in_=scr[0:1, 512:1024], func=AF.Sqrt,
                        bias=cst[0:1, 0:1], scale=1.0), s_sc)
                    P.wait("vector", s_sc)
                    P.op("vector", lambda e: e.reciprocal(
                        out=scr[0:1, 512:1024], in_=scr[0:1, 1024:1536]))
                    P.op("vector", lambda e: e.tensor_mul(
                        out=scr[0:1, 1536:2048], in0=scr[0:1, 1536:2048], in1=scr[0:1, 512:1024]))
                P.op("vector", lambda e: e.tensor_copy(
                    out=scr[0:1, 512:1024], in_=scr[0:1, 0:512]), s_ve)
                P.wait("tensor", s_ve)
                P.op("tensor", lambda e: e.matmul(
                    psA[:, :], ones_f[0:1, :], scr[0:1, 512:1024], start=True, stop=True))
                P.op("tensor", lambda e: e.matmul(
                    psB[:, :], ones_f[0:1, :], scr[0:1, 1536:2048], start=True, stop=True), s_mm)
                P.wait("vector", s_mm)
                for c in range(NC):
                    xc = x_fm[:, c * TOK + tt * 512: c * TOK + (tt + 1) * 512]
                    oc = out_t[:, c * occ + tt * 512: c * occ + (tt + 1) * 512]
                    sl = slice(0, 512) if c % 2 == 0 else slice(1024, 1536)
                    if affine:
                        if c >= 2:
                            P.wait("vector", s_sc)
                        P.op("vector", lambda e, xc=xc, sl=sl: e.tensor_sub(
                            out=scr[:, sl], in0=xc, in1=psA[:, :]))
                        P.op("vector", lambda e, sl=sl: e.tensor_mul(
                            out=scr[:, sl], in0=scr[:, sl], in1=psB[:, :]), s_ve)
                        P.wait("scalar", s_ve)
                        P.op("scalar", lambda e, c=c, oc=oc, sl=sl: e.activation(
                            out=oc, in_=scr[:, sl], func=AF.Identity,
                            bias=_pp(lnp, NC + c), scale=_pp(lnp, c)), s_sc)
                        if yspill is not None:
                            psC, ystg, yex = yspill
                            if tt > 0 and c < 2 and yex.get(tt - 1) is not None:
                                P.wait("vector", s_ex, yex[tt - 1])
                            P.op("vector", lambda e, xc=xc, c=c: e.tensor_sub(
                                out=ystg[:, c * 512:(c + 1) * 512], in0=xc, in1=psA[:, :]))
                            P.op("vector", lambda e, c=c: e.tensor_mul(
                                out=ystg[:, c * 512:(c + 1) * 512],
                                in0=ystg[:, c * 512:(c + 1) * 512], in1=psC[:, :]), s_ve)
                    else:
                        P.op("vector", lambda e, xc=xc, sl=sl: e.tensor_sub(
                            out=scr[:, sl], in0=xc, in1=psA[:, :]))
                        P.op("vector", lambda e, oc=oc, sl=sl: e.tensor_mul(
                            out=oc, in0=scr[:, sl], in1=psB[:, :]), s_ve)
                if yspill is not None:
                    psC, ystg, yex = yspill
                    P.wait("gpsimd", s_ve)
                    yex[tt] = P.dma("gpsimd", s_ex, y0_d.ap()[tt], ystg[:, :])
                P.wait("tensor", s_ve)
                if affine:
                    P.wait("vector", s_sc)

        def dequant_one(oi_t, io_t, dst_d, O, IN, tag):
            NOT = O // 128
            INB = IN // 32
            NCI = IN // 128
            phase_barrier()
            with ExitStack() as dx:
                DE = dx.enter_context
                psA = DE(nc.psum_tensor(f"dq_psA{tag}", [128, 512], F32))
                psB = DE(nc.psum_tensor(f"dq_psB{tag}", [128, 512], F32))
                wtf = DE(nc.sbuf_tensor(f"dq_wtf{tag}", [128, 3072], F32))
                sA = DE(nc.sbuf_tensor(f"dq_sA{tag}", [128, 768], F32))
                rsA = DE(nc.sbuf_tensor(f"dq_rsA{tag}", [128, 768], F32))
                tstg = DE(nc.sbuf_tensor(f"dq_tstg{tag}", [96, 128], F32))
                sT4 = DE(nc.sbuf_tensor(f"dq_sT4{tag}", [4, O], F32))
                rsT4 = DE(nc.sbuf_tensor(f"dq_rsT4{tag}", [4, O], F32))
                wqb = DE(nc.sbuf_tensor(f"dq_wqb{tag}", [128, O], BF16))
                for ot in range(NOT):
                    P.wait("sync", s_ve)
                    mk = P.dma("sync", s_hl, wtf[:, 0:IN], oi_t.ap()[ot * 128:(ot + 1) * 128, :])
                    P.wait("vector", s_hl, mk)
                    src = wtf.ap()[:, 0:IN].rearrange("p (b i) -> p b i", i=32)
                    P.op("vector", lambda e, src=src, ot=ot, INB=INB: e.tensor_reduce(
                        out=sA[:, ot * INB:(ot + 1) * INB], in_=src, axis=AX.X,
                        op=ALU.max, apply_absolute_value=True), s_ve)
                P.op("vector", lambda e, NOT=NOT, INB=INB: e.tensor_scalar(
                    out=sA[:, 0:NOT * INB], in0=sA[:, 0:NOT * INB], scalar1=1.0 / 7.0,
                    scalar2=1e-12, op0=ALU.mult, op1=ALU.max))
                P.op("vector", lambda e, NOT=NOT, INB=INB: e.reciprocal(
                    out=rsA[:, 0:NOT * INB], in_=sA[:, 0:NOT * INB]), s_ve)
                P.wait("tensor", s_ve)
                for ot in range(NOT):
                    for which in range(2):
                        srcT = sA if which == 0 else rsA
                        dstT = sT_d if which == 0 else rsT_d
                        ps = psA if which == 0 else psB
                        P.op("tensor", lambda e, ot=ot, srcT=srcT, ps=ps, INB=INB: e.transpose(
                            ps[0:INB, 0:128], srcT[:, ot * INB:(ot + 1) * INB], ident[:, :]), s_mm)
                        P.wait("vector", s_mm)
                        P.wait("vector", s_ex)
                        P.op("vector", lambda e, ps=ps, INB=INB: e.tensor_copy(
                            out=tstg[0:INB, :], in_=ps[0:INB, 0:128]), s_ve)
                        P.wait("tensor", s_ve)
                        P.wait("gpsimd", s_ve)
                        P.dma("gpsimd", s_ex, dstT.ap()[0:INB, ot * 128:(ot + 1) * 128],
                              tstg[0:INB, :])
                P.wait("sync", s_ex)
                for c in range(NCI):
                    P.wait("sync", s_ve)
                    P.wait("sync", s_ex)
                    mk = P.dma("sync", s_hl, wtf[:, 0:O], io_t.ap()[c * 128:(c + 1) * 128, :])
                    mk = P.dma("sync", s_hl, sT4[:, :], sT_d.ap()[4 * c:4 * c + 4, 0:O], fresh=False)
                    mk = P.dma("sync", s_hl, rsT4[:, :], rsT_d.ap()[4 * c:4 * c + 4, 0:O], fresh=False)
                    P.wait("tensor", s_hl, mk)
                    P.wait("vector", s_hl, mk)
                    for sl in range(O // 512):
                        P.wait("tensor", s_ve)
                        P.op("tensor", lambda e, sl=sl: e.matmul(
                            psA[:, :], rep4[:, :], rsT4[0:4, sl * 512:(sl + 1) * 512],
                            start=True, stop=True))
                        P.op("tensor", lambda e, sl=sl: e.matmul(
                            psB[:, :], rep4[:, :], sT4[0:4, sl * 512:(sl + 1) * 512],
                            start=True, stop=True), s_mm)
                        P.wait("vector", s_mm)
                        P.op("vector", lambda e, sl=sl: e.tensor_mul(
                            out=wtf[:, sl * 512:(sl + 1) * 512],
                            in0=wtf[:, sl * 512:(sl + 1) * 512], in1=psA[:, :]))
                        P.op("vector", lambda e, sl=sl: e.tensor_scalar(
                            out=wtf[:, sl * 512:(sl + 1) * 512],
                            in0=wtf[:, sl * 512:(sl + 1) * 512],
                            scalar1=MAGIC, scalar2=-MAGIC, op0=ALU.add, op1=ALU.add))
                        P.op("vector", lambda e, sl=sl: e.tensor_mul(
                            out=wqb[:, sl * 512:(sl + 1) * 512],
                            in0=wtf[:, sl * 512:(sl + 1) * 512], in1=psB[:, :]), s_ve)
                    P.wait("gpsimd", s_ve)
                    P.dma("gpsimd", s_ex, dst_d.ap()[c], wqb[:, 0:O])

        def dequant_weights():
            dequant_one(di["l1_wup_oi"], di["l1_wup_io"], wq_up_d, 3072, D, "u")
            dequant_one(di["l1_wgate_oi"], di["l1_wgate_io"], wq_gate_d, 3072, D, "g")
            dequant_one(di["l1_wdown_oi"], di["l1_wdown_io"], wq_down_d, D, 3072, "d")

        def attention(layer):
            phase_barrier()
            lt = f"l{layer}"
            with ExitStack() as ax:
                AE = ax.enter_context
                qall = AE(nc.sbuf_tensor(f"{lt}_qall", [128, HP * TOK], BF16))
                if layer == 0:
                    di_ropeC_dst = AE(nc.sbuf_tensor("ropeC_sb", [128, TOK], BF16))
                    di_ropeS_dst = AE(nc.sbuf_tensor("ropeS_sb", [128, TOK], BF16))
                m_own = AE(nc.sbuf_tensor(f"{lt}_mown", [128, 8], F32))
                scr = AE(nc.sbuf_tensor(f"{lt}_scr", [128, 2048], F32))

                P.wait("sync", s_ve)
                P.wait("sync", s_sc)
                P.dma("sync", s_hl, lnp[:, :], di[f"{lt}_lnp"].ap())
                if layer == 1:
                    P.dma("sync", s_hl, qkb[:, :], di["l1_b"].ap(), fresh=False)
                else:
                    P.dma("sync", s_hl, di_ropeC_dst[:, :], di["ropeC"].ap(), fresh=False)
                    P.dma("sync", s_hl, di_ropeS_dst[:, :], di["ropeS"].ap(), fresh=False)
                P.wait("vector", s_hl)
                P.wait("scalar", s_hl)

                with ExitStack() as px:
                    PE = px.enter_context
                    psA = PE(nc.psum_tensor(f"{lt}_psA", [128, 512], F32))
                    psB = PE(nc.psum_tensor(f"{lt}_psB", [128, 512], F32))
                    if layer == 0:
                        with nc.psum_tensor("l0_psC", [128, 512], F32) as psC, \
                             nc.sbuf_tensor("l0_ystg", [128, NC * 512], F32) as ystg:
                            fused_ln(True, h_bf, TOK, psA, psB, scr, yspill=(psC, ystg, {}))
                            P.wait("gpsimd", s_ex)
                            P.wait("sync", s_ex)
                    else:
                        fused_ln(True, h_bf, TOK, psA, psB, scr)
                    if layer == 0:
                        dump("dbg_h", h_bf[:, :])
                    wbuf = PE(nc.sbuf_tensor(f"{lt}_wbuf", [128, 4 * 1024], BF16))
                    ktmp = PE(nc.sbuf_tensor(f"{lt}_ktmp", [128, 2 * TOK], BF16))
                    kstg = PE(nc.sbuf_tensor(f"{lt}_kstg", [128, 2 * 256], BF16))
                    vstg = PE(nc.sbuf_tensor(f"{lt}_vstg", [128, 2 * 1040], BF16))
                    qtmp = PE(nc.sbuf_tensor(f"{lt}_qtmp", [128, 512], BF16))

                    for r in range(2):
                        ap = vstg.ap()[:, r * 1040:(r + 1) * 1040].rearrange(
                            "p (h n) -> p h n", n=65)[:, :, 64:65]
                        P.op("vector", lambda e, ap=ap: e.memset(ap, 1.0), s_ms)
                    P.wait("vector", s_ms)

                    def kv_export_k(g4):
                        ex_marks = {}
                        for tile in range(16):
                            r = tile % 2
                            if tile >= 2:
                                P.wait("vector", s_e[r], ex_marks[tile - 2])
                            for hpi in range(2):
                                P.op("vector", lambda e, r=r, hpi=hpi, tile=tile: e.tensor_copy(
                                    out=kstg[:, r * 256 + hpi * 128: r * 256 + (hpi + 1) * 128],
                                    in_=ktmp[:, hpi * TOK + tile * 128: hpi * TOK + tile * 128 + 128]),
                                    s_ve)
                            P.wait("gpsimd", s_ve)
                            ex_marks[tile] = P.dma("gpsimd", s_e[r], kv_k(kv_in, None, g4, tile),
                                                   kstg[:, r * 256:(r + 1) * 256])
                        P.wait("vector", s_e[0])
                        P.wait("vector", s_e[1])

                    if layer == 0:
                        ropeC, ropeS = di_ropeC_dst, di_ropeS_dst
                        cq = PE(nc.sbuf_tensor("l0_cq", [128, 4 * TOK], BF16))
                        ckv = PE(nc.sbuf_tensor("l0_ckv", [128, 4 * TOK], BF16))

                        for wname, dst in (("l0_wdq", cq), ("l0_wdkv", ckv)):
                            P.wait("gpsimd", s_mm)
                            for c in range(NC):
                                P.dma("gpsimd", s_wl, wbuf[:, c * 512:(c + 1) * 512],
                                      di[wname].ap()[c * 128:(c + 1) * 128, :], fresh=(c == 0))
                            P.wait("tensor", s_wl)
                            P.wait("tensor", s_sc)
                            for lc in range(4):
                                for tt in range(NT):
                                    for c in range(NC):
                                        P.op("tensor", lambda e, lc=lc, tt=tt, c=c: e.matmul(
                                            psA[:, :],
                                            wbuf[:, c * 512 + lc * 128: c * 512 + (lc + 1) * 128],
                                            h_bf[:, c * TOK + tt * 512: c * TOK + (tt + 1) * 512],
                                            start=(c == 0), stop=(c == NC - 1)),
                                            s_mm if c == NC - 1 else None)
                                    P.wait("scalar", s_mm)
                                    P.op("scalar", lambda e, lc=lc, tt=tt, dst=dst: e.activation(
                                        out=dst[:, lc * TOK + tt * 512: lc * TOK + (tt + 1) * 512],
                                        in_=psA[:, :], func=AF.Copy), s_sc)
                                    P.wait("tensor", s_sc)

                        def rope_emit(outslice, tt):
                            P.op("scalar", lambda e: e.activation(
                                out=qtmp[:, :], in_=psA[:, :], func=AF.Copy), s_sc)
                            P.wait("tensor", s_sc)
                            P.op("tensor", lambda e: e.matmul(
                                psB[:, :], permT[:, :], qtmp[:, :], start=True, stop=True), s_mm)
                            P.wait("vector", s_mm)
                            tc = ropeC[:, tt * 512:(tt + 1) * 512]
                            ts = ropeS[:, tt * 512:(tt + 1) * 512]
                            P.op("vector", lambda e, tc=tc: e.tensor_mul(
                                out=scr[:, 0:512], in0=qtmp[:, :], in1=tc))
                            P.op("vector", lambda e, ts=ts: e.tensor_mul(
                                out=scr[:, 512:1024], in0=psB[:, :], in1=ts))
                            P.op("vector", lambda e, outslice=outslice: e.tensor_add(
                                out=outslice, in0=scr[:, 0:512], in1=scr[:, 512:1024]), s_ve)
                            P.wait("tensor", s_ve)
                            P.wait("scalar", s_ve)

                        dump("dbg_cq", cq[:, :])
                        dump("dbg_ckv", ckv[:, :])
                        P.wait("gpsimd", s_mm)
                        for c in range(4):
                            P.dma("gpsimd", s_wl, wbuf[:, c * 1024:(c + 1) * 1024],
                                  di["l0_wuq"].ap()[c * 128:(c + 1) * 128, :], fresh=(c == 0))
                        P.wait("tensor", s_wl)
                        for hp in range(HP):
                            for tt in range(NT):
                                for c in range(4):
                                    P.op("tensor", lambda e, hp=hp, tt=tt, c=c: e.matmul(
                                        psA[:, :],
                                        wbuf[:, c * 1024 + hp * 128: c * 1024 + (hp + 1) * 128],
                                        cq[:, c * TOK + tt * 512: c * TOK + (tt + 1) * 512],
                                        start=(c == 0), stop=(c == 3)), s_mm if c == 3 else None)
                                P.wait("scalar", s_mm)
                                rope_emit(qall[:, hp * TOK + tt * 512: hp * TOK + (tt + 1) * 512], tt)

                        P.wait("gpsimd", s_mm)
                        for c in range(4):
                            P.dma("gpsimd", s_wl, wbuf[:, c * 1024:(c + 1) * 1024],
                                  di["l0_wuk"].ap()[c * 128:(c + 1) * 128, :], fresh=(c == 0))
                        P.wait("tensor", s_wl)
                        for g4 in range(4):
                            for hpi in range(2):
                                hp = g4 * 2 + hpi
                                for tt in range(NT):
                                    for c in range(4):
                                        P.op("tensor", lambda e, hp=hp, tt=tt, c=c: e.matmul(
                                            psA[:, :],
                                            wbuf[:, c * 1024 + hp * 128: c * 1024 + (hp + 1) * 128],
                                            ckv[:, c * TOK + tt * 512: c * TOK + (tt + 1) * 512],
                                            start=(c == 0), stop=(c == 3)), s_mm if c == 3 else None)
                                    P.wait("scalar", s_mm)
                                    rope_emit(ktmp[:, hpi * TOK + tt * 512: hpi * TOK + (tt + 1) * 512], tt)
                                P.op("vector", lambda e, hp=hp, hpi=hpi: e.tensor_reduce(
                                    out=_pp(m_own, hp), in_=ktmp[:, hpi * TOK:(hpi + 1) * TOK],
                                    axis=AX.X, op=ALU.max, apply_absolute_value=True), s_ve)
                            kv_export_k(g4)

                        P.wait("gpsimd", s_mm)
                        for c in range(4):
                            P.dma("gpsimd", s_wl, wbuf[:, c * 1024:(c + 1) * 1024],
                                  di["l0_wuv"].ap()[c * 128:(c + 1) * 128, :], fresh=(c == 0))
                        P.wait("tensor", s_wl)
                        vex_marks = {}
                        for vt in range(16):
                            r = vt % 2
                            for half in range(2):
                                ps = psA if half == 0 else psB
                                for c in range(4):
                                    P.op("tensor", lambda e, vt=vt, half=half, c=c, ps=ps: e.matmul(
                                        ps[:, :],
                                        ckv[:, c * TOK + vt * 128: c * TOK + (vt + 1) * 128],
                                        wbuf[:, c * 1024 + half * 512: c * 1024 + (half + 1) * 512],
                                        start=(c == 0), stop=(c == 3)), s_mm if c == 3 else None)
                            P.wait("vector", s_mm)
                            if vt >= 2:
                                P.wait("vector", s_e[r], vex_marks[vt - 2])
                            for half in range(2):
                                ps = psA if half == 0 else psB
                                dst = vstg.ap()[:, r * 1040:(r + 1) * 1040].rearrange(
                                    "p (h n) -> p h n", n=65)[:, half * 8:(half + 1) * 8, 0:64]
                                src = ps.ap().rearrange("p (h n) -> p h n", n=64)
                                P.op("vector", lambda e, dst=dst, src=src: e.tensor_copy(
                                    out=dst, in_=src), s_ve)
                            P.wait("gpsimd", s_ve)
                            vex_marks[vt] = P.dma("gpsimd", s_e[r], kv_v(kv_in, None, vt),
                                                  vstg[:, r * 1040:(r + 1) * 1040])
                            P.wait("tensor", s_ve)
                        P.wait("vector", s_e[0])
                        P.wait("vector", s_e[1])
                    else:
                        wq_marks = {}
                        for hp in range(HP):
                            woff = (hp % 2) * 1024
                            if hp >= 2:
                                P.wait("gpsimd", s_mm, wq_marks[hp - 2])
                            wqlm = P.dma("gpsimd", s_w[hp % 2], wbuf[:, woff:woff + 1024],
                                  di["l1_wqt"].ap()[hp].rearrange("c p n -> p c n"))
                            P.wait("tensor", s_w[hp % 2], wqlm)
                            for tt in range(NT):
                                for c in range(NC):
                                    mk = P.op("tensor", lambda e, tt=tt, c=c, woff=woff: e.matmul(
                                        psA[:, :],
                                        wbuf[:, woff + c * 128: woff + (c + 1) * 128],
                                        h_bf[:, c * TOK + tt * 512: c * TOK + (tt + 1) * 512],
                                        start=(c == 0), stop=(c == NC - 1)), s_mm if c == NC - 1 else None)
                                wq_marks[hp] = mk
                                P.wait("scalar", s_mm)
                                P.op("scalar", lambda e, hp=hp, tt=tt: e.activation(
                                    out=qall[:, hp * TOK + tt * 512: hp * TOK + (tt + 1) * 512],
                                    in_=psA[:, :], func=AF.Identity, bias=_pp(qkb, hp), scale=1.0), s_sc)
                                P.wait("tensor", s_sc)
                        for g4 in range(4):
                            for hpi in range(2):
                                hp = g4 * 2 + hpi
                                P.wait("gpsimd", s_mm)
                                wklm = P.dma("gpsimd", s_w[hpi], wbuf[:, hpi * 1024:(hpi + 1) * 1024],
                                      di["l1_wkt"].ap()[hp].rearrange("c p n -> p c n"))
                                P.wait("tensor", s_w[hpi], wklm)
                                for tt in range(NT):
                                    for c in range(NC):
                                        P.op("tensor", lambda e, hpi=hpi, tt=tt, c=c: e.matmul(
                                            psA[:, :],
                                            wbuf[:, hpi * 1024 + c * 128: hpi * 1024 + (c + 1) * 128],
                                            h_bf[:, c * TOK + tt * 512: c * TOK + (tt + 1) * 512],
                                            start=(c == 0), stop=(c == NC - 1)), s_mm if c == NC - 1 else None)
                                    P.wait("scalar", s_mm)
                                    P.op("scalar", lambda e, hpi=hpi, tt=tt, hp=hp: e.activation(
                                        out=ktmp[:, hpi * TOK + tt * 512: hpi * TOK + (tt + 1) * 512],
                                        in_=psA[:, :], func=AF.Identity, bias=_pp(qkb, NC + hp), scale=1.0), s_sc)
                                    P.wait("tensor", s_sc)
                                P.wait("vector", s_sc)
                                P.op("vector", lambda e, hp=hp, hpi=hpi: e.tensor_reduce(
                                    out=_pp(m_own, hp), in_=ktmp[:, hpi * TOK:(hpi + 1) * TOK],
                                    axis=AX.X, op=ALU.max, apply_absolute_value=True), s_ve)
                            kv_export_k(g4)
                        bvrow = PE(nc.sbuf_tensor("l1_bvrow", [1, 1024], F32))
                        bvb = PE(nc.sbuf_tensor("l1_bvb", [128, 1024], F32))
                        P.dma("sync", s_hl, bvrow[:, :], di["l1_bv_row"].ap())
                        P.wait("tensor", s_hl)
                        P.wait("tensor", s_ve)
                        for half in range(2):
                            P.op("tensor", lambda e, half=half: e.matmul(
                                psB[:, :], ones_f[0:1, :], bvrow[0:1, half * 512:(half + 1) * 512],
                                start=True, stop=True), s_mm)
                            P.wait("vector", s_mm)
                            P.op("vector", lambda e, half=half: e.tensor_copy(
                                out=bvb[:, half * 512:(half + 1) * 512], in_=psB[:, :]), s_ve)
                            P.wait("tensor", s_ve)
                        vex_marks = {}
                        for vt in range(16):
                            r = vt % 2
                            for chalf in range(2):
                                P.wait("gpsimd", s_mm)
                                vwm = 0
                                for ci in range(4):
                                    c = chalf * 4 + ci
                                    vwm = P.dma("gpsimd", s_w[chalf], wbuf[:, ci * 1024:(ci + 1) * 1024],
                                          di["l1_wv"].ap()[c * 128:(c + 1) * 128, :], fresh=(ci == 0))
                                P.wait("tensor", s_w[chalf], vwm)
                                for half in range(2):
                                    ps = psA if half == 0 else psB
                                    for ci in range(4):
                                        c = chalf * 4 + ci
                                        P.op("tensor", lambda e, vt=vt, half=half, c=c, ci=ci, ps=ps, chalf=chalf: e.matmul(
                                            ps[:, :],
                                            h_bf[:, c * TOK + vt * 128: c * TOK + (vt + 1) * 128],
                                            wbuf[:, ci * 1024 + half * 512: ci * 1024 + (half + 1) * 512],
                                            start=(c == 0), stop=(c == NC - 1)),
                                            s_mm if (c == NC - 1 or c == 3) else None)
                            P.wait("vector", s_mm)
                            if vt >= 2:
                                P.wait("vector", s_e[r], vex_marks[vt - 2])
                            for half in range(2):
                                ps = psA if half == 0 else psB
                                dst = vstg.ap()[:, r * 1040:(r + 1) * 1040].rearrange(
                                    "p (h n) -> p h n", n=65)[:, half * 8:(half + 1) * 8, 0:64]
                                src = ps.ap().rearrange("p (h n) -> p h n", n=64)
                                bsl = bvb.ap()[:, half * 512:(half + 1) * 512].rearrange(
                                    "p (h n) -> p h n", n=64)
                                P.op("vector", lambda e, dst=dst, src=src, bsl=bsl: e.tensor_add(
                                    out=dst, in0=src, in1=bsl), s_ve)
                            P.wait("gpsimd", s_ve)
                            vex_marks[vt] = P.dma("gpsimd", s_e[r], kv_v(kv_in, None, vt),
                                                  vstg[:, r * 1040:(r + 1) * 1040])
                            P.wait("tensor", s_ve)
                        P.wait("vector", s_e[0])
                        P.wait("vector", s_e[1])

                    P.wait("gpsimd", s_ve)
                    P.dma("gpsimd", s_ex, kv_m(kv_in, None), m_own[:, :])
                    if layer == 0:
                        dump("dbg_q", qall[:, :])
                        dump("dbg_kv", kv_in.ap())

                P.wait("gpsimd", s_ex)
                P.wait("gpsimd", s_e[0])
                P.wait("gpsimd", s_e[1])
                P.wait("gpsimd", s_hl)
                P.wait("gpsimd", s_mm)
                P.op("gpsimd", lambda e: e.collective_compute(
                    "AllGather", ALU.bypass,
                    replica_groups=[[2 * i, 2 * i + 1] for i in range(world // 2)],
                    ins=[kv_in.ap().opt()], outs=[kv_out.ap().opt()]), s_cc)
                CC = P.val(s_cc)

                if layer == 1 and n_stages >= 4:
                    dequant_weights()

                P.wait("sync", s_cc, CC)
                phase_barrier()

                with ExitStack() as cx:
                    CE = cx.enter_context
                    kbuf = CE(nc.sbuf_tensor(f"{lt}_kbuf", [128, 2 * 256], BF16))
                    vbuf = CE(nc.sbuf_tensor(f"{lt}_vbuf", [128, 2 * 260], BF16))
                    ptb = CE(nc.sbuf_tensor(f"{lt}_ptb", [128, 4 * 512], BF16))
                    o_qg = CE(nc.sbuf_tensor(f"{lt}_oqg", [64, H * 512], BF16))
                    wo_sb = CE(nc.sbuf_tensor(f"{lt}_wosb", [64, 2 * 2048], BF16))
                    qa = CE(nc.sbuf_tensor(f"{lt}_qa", [128, TOK], BF16))
                    mk2 = CE(nc.sbuf_tensor(f"{lt}_mk2", [128, 16], BF16))
                    m_k = CE(nc.sbuf_tensor(f"{lt}_mk", [128, 16], F32))
                    bias_t = CE(nc.sbuf_tensor(f"{lt}_bias", [128, 64], F32))
                    nb2 = CE(nc.sbuf_tensor(f"{lt}_nb2", [2, 4], F32))
                    sigs = CE(nc.sbuf_tensor(f"{lt}_sigs", [65, 1536], F32))
                    recb = CE(nc.sbuf_tensor(f"{lt}_recb", [64, 512], F32))
                    mbuf = CE(nc.sbuf_tensor(f"{lt}_mbuf", [128, 8 * 512], BF16))
                    ybk = CE(nc.sbuf_tensor(f"{lt}_ybk", [128, 2 * 512], F32))
                    psOT = [CE(nc.psum_tensor(f"{lt}_psOT{i}", [65, 512], F32)) for i in range(4)]
                    psST = [CE(nc.psum_tensor(f"{lt}_psST{i}", [128, 512], F32)) for i in range(4)]
                    if layer == 1:
                        borow = CE(nc.sbuf_tensor("l1_borow", [1, 1024], F32))
                        P.dma("sync", s_hl, borow[:, :], di["l1_bo_row"].ap())


                    P.dma("sync", s_hl, m_k[:, 0:8], kv_m(kv_out, 0))
                    P.dma("sync", s_hl, m_k[:, 8:16], kv_m(kv_out, 1), fresh=False)
                    P.wait("vector", s_hl)
                    P.op("vector", lambda e: e.tensor_max(
                        out=m_k[:, 0:8], in0=m_k[:, 0:8], in1=m_k[:, 8:16]))
                    P.op("vector", lambda e: e.memset(mk2[:, :], 0.0), s_ms)
                    P.wait("vector", s_ms)
                    for hp in range(HP):
                        P.op("vector", lambda e, hp=hp: e.tensor_scalar(
                            out=mk2[0:64, 2 * hp:2 * hp + 1], in0=m_k[0:64, hp:hp + 1],
                            scalar1=0.125, scalar2=None, op0=ALU.mult))
                        P.op("vector", lambda e, hp=hp: e.tensor_scalar(
                            out=mk2[64:128, 2 * hp + 1:2 * hp + 2], in0=m_k[64:128, hp:hp + 1],
                            scalar1=0.125, scalar2=None, op0=ALU.mult), s_ve)

                    P.wait("scalar", s_ve)
                    for hp in range(HP):
                        P.op("scalar", lambda e, hp=hp: e.activation(
                            out=qa[:, :], in_=qall[:, hp * TOK:(hp + 1) * TOK],
                            func=AF.Abs), s_sc)
                        P.wait("tensor", s_sc)
                        P.wait("tensor", s_ve)
                        for qg in range(NT):
                            P.op("tensor", lambda e, hp=hp, qg=qg: e.matmul(
                                psST[qg][0:2, :], mk2[:, 2 * hp:2 * hp + 2],
                                qa[:, qg * 512:(qg + 1) * 512], start=True, stop=True),
                                s_mm if qg == NT - 1 else None)
                        P.wait("vector", s_mm)
                        for qg in range(NT):
                            P.op("vector", lambda e, qg=qg: e.tensor_reduce(
                                out=nb2[0:2, qg:qg + 1], in_=psST[qg][0:2, :], axis=AX.X,
                                op=ALU.max, negate=True), s_ve if qg == NT - 1 else None)
                        P.wait("tensor", s_ve)
                        for hq in range(2):
                            P.op("tensor", lambda e, hq=hq: e.matmul(
                                psST[hq][:, 0:4], sel2[:, hq * 128:(hq + 1) * 128],
                                nb2[0:2, 0:4], start=True, stop=True), s_mm if hq == 1 else None)
                        P.wait("vector", s_mm)
                        for hq in range(2):
                            h = 2 * hp + hq
                            P.op("vector", lambda e, hq=hq, h=h: e.tensor_copy(
                                out=bias_t[:, h * 4:(h + 1) * 4], in_=psST[hq][:, 0:4]), s_ve)
                        P.wait("tensor", s_ve)
                        P.wait("scalar", s_ve)

                    if layer == 0:
                        dump("dbg_mk", m_k[:, :])
                        dump("dbg_bias", bias_t[:, :])
                    nj = (lambda qg: 32) if layer == 0 else (lambda qg: 8 * qg + 8)

                    prev_pv = [None, None]
                    xadd_marks = {}
                    wo_marks = {}
                    for qg in range(NT):
                        mask_mark = None
                        if layer == 1:
                            P.wait("sync", s_ve)
                            mask_mark = P.dma("sync", s_hl, mbuf[:, :],
                                  di["l1_masks"].ap()[qg].rearrange("j p n -> p j n"))
                        for g4 in range(4):
                            njq = nj(qg)
                            ld_marks = {}
                            pv_marks = {}
                            P.wait("tensor", s_ve)
                            for j in range(njq):
                                slot, loc = j % 2, j // 2
                                if j >= 2:
                                    P.wait("sync", s_mm, pv_marks[j - 2])
                                elif prev_pv[j % 2] is not None:
                                    P.wait("sync", s_mm, prev_pv[j % 2])
                                P.dma("sync", s_kv[j % 2], kbuf[:, (j % 2) * 256:(j % 2 + 1) * 256],
                                      kv_k(kv_out, slot, g4, loc))
                                ld_marks[j] = P.dma(
                                    "sync", s_kv[j % 2], vbuf[:, (j % 2) * 260:(j % 2 + 1) * 260],
                                    kv_v(kv_out, slot, loc)[:, g4 * 260:(g4 + 1) * 260], fresh=False)
                                kb = kbuf[:, (j % 2) * 256:(j % 2 + 1) * 256]
                                vb = vbuf[:, (j % 2) * 260:(j % 2 + 1) * 260]
                                P.wait("tensor", s_kv[j % 2], ld_marks[j])
                                P.wait("tensor", s_sc)
                                for hi in range(4):
                                    hpi, hq = hi // 2, hi % 2
                                    hp = g4 * 2 + hpi
                                    r0 = hq * 64
                                    P.op("tensor", lambda e, hi=hi, hpi=hpi, r0=r0, hp=hp, qg=qg, kb=kb: e.matmul(
                                        psST[hi][:, :],
                                        kb[r0:r0 + 64, hpi * 128:(hpi + 1) * 128],
                                        qall[r0:r0 + 64, hp * TOK + qg * 512: hp * TOK + (qg + 1) * 512],
                                        start=True, stop=True), s_mm if hi == 3 else None)
                                P.wait("scalar", s_mm)
                                masked = (layer == 1) and (j >= 8 * qg)
                                for hi in range(4):
                                    hpi, hq = hi // 2, hi % 2
                                    h = g4 * 4 + 2 * hpi + hq
                                    P.op("scalar", lambda e, hi=hi, h=h, qg=qg: e.activation(
                                        out=ptb[:, hi * 512:(hi + 1) * 512], in_=psST[hi][:, :],
                                        func=AF.Exp, bias=_pp(bias_t, h * 4 + qg),
                                        scale=0.125), s_sc)
                                if masked:
                                    jj = j - 8 * qg
                                    P.wait("vector", s_hl, mask_mark)
                                    P.wait("vector", s_sc)
                                    for hi in range(4):
                                        P.op("vector", lambda e, hi=hi, jj=jj: e.tensor_mul(
                                            out=ptb[:, hi * 512:(hi + 1) * 512],
                                            in0=ptb[:, hi * 512:(hi + 1) * 512],
                                            in1=mbuf[:, jj * 512:(jj + 1) * 512]), s_ve)
                                    P.wait("tensor", s_ve)
                                else:
                                    P.wait("tensor", s_sc)
                                for hi in range(4):
                                    hpi, hq = hi // 2, hi % 2
                                    hloc = 2 * hpi + hq
                                    mk = P.op("tensor", lambda e, hi=hi, hloc=hloc, vb=vb, j=j, njq=njq: e.matmul(
                                        psOT[hloc][:, :],
                                        vb[:, hloc * 65:(hloc + 1) * 65],
                                        ptb[:, hi * 512:(hi + 1) * 512],
                                        start=(j == 0), stop=(j == njq - 1)),
                                        s_mm if hi == 3 else None)
                                pv_marks[j] = mk
                            prev_pv = [pv_marks.get(njq - 2), pv_marks.get(njq - 1)]
                            if njq - 1 >= 0 and (njq - 1) % 2 == 0:
                                prev_pv = [pv_marks.get(njq - 1), pv_marks.get(njq - 2)]
                            P.wait("vector", s_mm)
                            for hloc in range(4):
                                h = g4 * 4 + hloc
                                P.op("vector", lambda e, hloc=hloc: e.tensor_copy(
                                    out=sigs[64:65, 0:512], in_=psOT[hloc][64:65, :]), s_ve)
                                P.wait("scalar", s_ve)
                                P.op("scalar", lambda e: e.activation(
                                    out=sigs[64:65, 512:1024], in_=sigs[64:65, 0:512], func=AF.Ln))
                                P.op("scalar", lambda e: e.activation(
                                    out=sigs[64:65, 1024:1536], in_=sigs[64:65, 512:1024],
                                    func=AF.Exp, scale=-1.0), s_sc)
                                P.wait("tensor", s_sc)
                                P.op("tensor", lambda e: e.matmul(
                                    psST[0][0:64, :], ones_f[64:65, 0:64], sigs[64:65, 1024:1536],
                                    start=True, stop=True), s_mm)
                                P.wait("vector", s_mm)
                                P.op("vector", lambda e: e.tensor_copy(
                                    out=recb[:, :], in_=psST[0][0:64, :]))
                                P.op("vector", lambda e, hloc=hloc, h=h: e.tensor_mul(
                                    out=o_qg[0:64, h * 512:(h + 1) * 512],
                                    in0=psOT[hloc][0:64, :], in1=recb[:, :]), s_ve)
                                P.wait("tensor", s_ve)
                                P.wait("scalar", s_ve)
                        if layer == 0:
                            dump("dbg_oqg" + ("" if qg == 0 else str(qg)), o_qg[:, :])
                        if layer == 1:
                            P.wait("tensor", s_hl)
                        P.wait("tensor", s_ve)
                        for oc in range(NC):
                            r = oc % 2
                            wk = wo_marks.get((qg, oc - 2), wo_marks.get((qg - 1, oc + 6)))
                            if wk is not None:
                                P.wait("gpsimd", s_mm, wk)
                            if layer == 0:
                                src = di["l0_wo"].ap().rearrange(
                                    "(h r) n -> r h n", r=64)[:, :, oc * 128:(oc + 1) * 128]
                            else:
                                src = di["l1_wo"].ap().rearrange(
                                    "h r n -> r h n")[:, :, oc * 128:(oc + 1) * 128]
                            wom = P.dma("gpsimd", s_w[r], wo_sb[0:64, r * 2048:(r + 1) * 2048], src)
                            P.wait("tensor", s_w[r], wom)
                            pso = psST[1 + r]
                            prev_add = xadd_marks.get((qg, oc - 2), xadd_marks.get((qg - 1, oc + 6)))
                            if prev_add is not None:
                                P.wait("tensor", s_ve, prev_add)
                            for hh in range(H):
                                last = (hh == H - 1) and (layer == 0)
                                wo_marks[(qg, oc)] = P.op("tensor", lambda e, hh=hh, r=r, pso=pso, last=last: e.matmul(
                                    pso[:, :],
                                    wo_sb[0:64, r * 2048 + hh * 128: r * 2048 + (hh + 1) * 128],
                                    o_qg[0:64, hh * 512:(hh + 1) * 512],
                                    start=(hh == 0), stop=last),
                                    s_mm if last else None)
                            if layer == 1:
                                wo_marks[(qg, oc)] = P.op("tensor", lambda e, oc=oc, pso=pso: e.matmul(
                                    pso[:, :], borow[0:1, oc * 128:(oc + 1) * 128],
                                    ones512[0:1, :], start=False, stop=True), s_mm)
                            P.wait("vector", s_mm)
                            xsl = x_fm[:, oc * TOK + qg * 512: oc * TOK + (qg + 1) * 512]
                            if layer == 0:
                                if prev_add is not None:
                                    P.wait("sync", s_ve, prev_add)
                                ylm = P.dma("sync", s_l[oc % 2],
                                            ybk[:, (oc % 2) * 512:(oc % 2 + 1) * 512],
                                            y0_d.ap()[qg][:, oc * 512:(oc + 1) * 512])
                                P.op("vector", lambda e, pso=pso, xsl=xsl: e.tensor_add(
                                    out=xsl, in0=pso[:, :], in1=xsl))
                                P.wait("vector", s_l[oc % 2], ylm)
                                xadd_marks[(qg, oc)] = P.op(
                                    "vector", lambda e, xsl=xsl, oc=oc: e.tensor_add(
                                        out=xsl, in0=ybk[:, (oc % 2) * 512:(oc % 2 + 1) * 512],
                                        in1=xsl), s_ve)
                            else:
                                xadd_marks[(qg, oc)] = P.op("vector", lambda e, pso=pso, xsl=xsl: e.tensor_add(
                                    out=xsl, in0=pso[:, :], in1=xsl), s_ve)

        def ffn0():
            phase_barrier()
            with ExitStack() as fx:
                FE = fx.enter_context
                scr = FE(nc.sbuf_tensor("f0_scr", [128, 2048], F32))
                psA = FE(nc.psum_tensor("f0_psA", [128, 512], F32))
                psB = FE(nc.psum_tensor("f0_psB", [128, 512], F32))
                fused_ln(False, h_bf, TOK, psA, psB, scr)
                fs = FE(nc.sbuf_tensor("f0_fs", [128, 136], F32))
                P.dma("sync", s_hl, fs[:, :], di["l0_fs"].ap())
                A = FE(nc.sbuf_tensor("f0_A", [128, 32 * 512], BF16))
                G = FE(nc.sbuf_tensor("f0_G", [128, 32 * 512], BF16))
                fwb = FE(nc.sbuf_tensor("f0_fwb", [128, 2 * 2048], BF16))
                fwb2 = FE(nc.sbuf_tensor("f0_fwb2", [128, 2 * 4096], BF16))
                tS = FE(nc.sbuf_tensor("f0_tS", [128, 2 * 512], BF16))
                tU = FE(nc.sbuf_tensor("f0_tU", [128, 2 * 512], BF16))
                psF = [FE(nc.psum_tensor(f"f0_psF{i}", [128, 512], F32)) for i in range(4)]
                P.wait("scalar", s_hl)
                P.wait("vector", s_hl)

                for tt in range(NT):
                    P.wait("gpsimd", s_mm)
                    P.wait("gpsimd", s_ve)
                    P.wait("gpsimd", s_sc)
                    mm_marks = {}
                    ve_marks = {}
                    for m in range(32):
                        r = m % 2
                        if m >= 2:
                            P.wait("gpsimd", s_mm, mm_marks[m - 2])
                        P.dma("gpsimd", s_w[r], fwb[:, r * 2048:r * 2048 + 1024],
                              di["l0_w1t"].ap()[m].rearrange("c p n -> p c n"))
                        wlm = P.dma("gpsimd", s_w[r], fwb[:, r * 2048 + 1024:(r + 1) * 2048],
                                    di["l0_w2t"].ap()[m].rearrange("c p n -> p c n"), fresh=False)
                        psU1, psU2 = psF[2 * r], psF[2 * r + 1]
                        P.wait("tensor", s_w[r], wlm)
                        if m >= 2:
                            P.wait("tensor", s_sc)
                        for c in range(NC):
                            P.op("tensor", lambda e, r=r, c=c, psU1=psU1, tt=tt: e.matmul(
                                psU1[:, :], fwb[:, r * 2048 + c * 128: r * 2048 + (c + 1) * 128],
                                h_bf[:, c * TOK + tt * 512: c * TOK + (tt + 1) * 512],
                                start=(c == 0), stop=(c == NC - 1)))
                        for c in range(NC):
                            mm_marks[m] = P.op("tensor", lambda e, r=r, c=c, psU2=psU2, tt=tt: e.matmul(
                                psU2[:, :], fwb[:, r * 2048 + 1024 + c * 128: r * 2048 + 1024 + (c + 1) * 128],
                                h_bf[:, c * TOK + tt * 512: c * TOK + (tt + 1) * 512],
                                start=(c == 0), stop=(c == NC - 1)),
                                s_mm if c == NC - 1 else None)
                        P.wait("scalar", s_mm)
                        if m >= 2:
                            P.wait("scalar", s_ve, ve_marks[m - 2])
                        P.op("scalar", lambda e, m=m, r=r, psU1=psU1: e.activation(
                            out=tS[:, r * 512:(r + 1) * 512], in_=psU1[:, :], func=AF.Silu,
                            bias=0.0, scale=_pp(fs, m)))
                        P.op("scalar", lambda e, m=m, r=r, psU2=psU2: e.activation(
                            out=tU[:, r * 512:(r + 1) * 512], in_=psU2[:, :], func=AF.Identity,
                            bias=0.0, scale=_pp(fs, 32 + m)), s_sc)
                        P.wait("vector", s_sc)
                        ve_marks[m] = P.op("vector", lambda e, m=m, r=r: e.tensor_mul(
                            out=A[:, m * 512:(m + 1) * 512],
                            in0=tS[:, r * 512:(r + 1) * 512], in1=tU[:, r * 512:(r + 1) * 512]), s_ve)
                    P.wait("tensor", s_ve)
                    for wt, scol, srcb, dstb in (("l0_h1t", 64, A, G), ("l0_h2t", 96, G, A)):
                        P.wait("gpsimd", s_mm)
                        P.wait("gpsimd", s_sc)
                        mm_marks = {}
                        sc_marks = {}
                        for m in range(32):
                            r = m % 2
                            if m >= 2:
                                P.wait("gpsimd", s_mm, mm_marks[m - 2])
                            wlm = P.dma("gpsimd", s_w[r], fwb2[:, r * 4096:(r + 1) * 4096],
                                        di[wt].ap()[m].rearrange("c p n -> p c n"))
                            ps = psF[r]
                            P.wait("tensor", s_w[r], wlm)
                            if m >= 2:
                                P.wait("tensor", s_sc, sc_marks[m - 2])
                            if wt == "l0_h2t" and m == 0:
                                P.wait("tensor", s_sc)
                            for c in range(32):
                                mm_marks[m] = P.op("tensor", lambda e, r=r, c=c, ps=ps, srcb=srcb: e.matmul(
                                    ps[:, :], fwb2[:, r * 4096 + c * 128: r * 4096 + (c + 1) * 128],
                                    srcb[:, c * 512:(c + 1) * 512],
                                    start=(c == 0), stop=(c == 31)), s_mm if c == 31 else None)
                            P.wait("scalar", s_mm)
                            sc_marks[m] = P.op("scalar", lambda e, m=m, ps=ps, scol=scol, dstb=dstb: e.activation(
                                out=dstb[:, m * 512:(m + 1) * 512], in_=ps[:, :], func=AF.Gelu,
                                bias=0.0, scale=_pp(fs, scol + m)), s_sc)
                    P.wait("tensor", s_sc)
                    P.wait("gpsimd", s_mm)
                    P.wait("gpsimd", s_sc)
                    mm_marks = {}
                    ve_marks = {}
                    for ocb in range(NC):
                        r = ocb % 2
                        if ocb >= 2:
                            P.wait("gpsimd", s_mm, mm_marks[ocb - 2])
                        wlm = P.dma("gpsimd", s_w[r], fwb2[:, r * 4096:(r + 1) * 4096],
                                    di["l0_w3t"].ap()[ocb].rearrange("c p n -> p c n"))
                        ps = psF[2 + r]
                        P.wait("tensor", s_w[r], wlm)
                        if ocb >= 2:
                            P.wait("tensor", s_ve, ve_marks[ocb - 2])
                        for c in range(32):
                            mm_marks[ocb] = P.op("tensor", lambda e, r=r, c=c, ps=ps: e.matmul(
                                ps[:, :], fwb2[:, r * 4096 + c * 128: r * 4096 + (c + 1) * 128],
                                A[:, c * 512:(c + 1) * 512],
                                start=(c == 0), stop=(c == 31)), s_mm if c == 31 else None)
                        P.wait("vector", s_mm)
                        xsl = x_fm[:, ocb * TOK + tt * 512: ocb * TOK + (tt + 1) * 512]
                        ve_marks[ocb] = P.op("vector", lambda e, ps=ps, xsl=xsl, ocb=ocb: e.scalar_tensor_tensor(
                            out=xsl, in0=ps[:, :], scalar=_pp(fs, 128 + ocb), in1=xsl,
                            op0=ALU.mult, op1=ALU.add), s_ve)

        def ffn1():
            phase_barrier()
            with ExitStack() as fx:
                FE = fx.enter_context
                psA = FE(nc.psum_tensor("f1_psA", [128, 512], F32))
                psB = FE(nc.psum_tensor("f1_psB", [128, 512], F32))
                scr = FE(nc.sbuf_tensor("f1_scr", [128, 2048], F32))

                def cross_part_max(src_col, dst_col):
                    P.wait("gpsimd", s_ve)
                    P.dma("gpsimd", s_ex, mrow[0:1, :], src_col)
                    P.wait("vector", s_ex)
                    P.op("vector", lambda e, dst_col=dst_col: e.tensor_reduce(
                        out=mxs[0:1, dst_col:dst_col + 1], in_=mrow[0:1, :], axis=AX.X,
                        op=ALU.max), s_ve)

                def allreduce_max(k):
                    P.wait("gpsimd", s_ve)
                    P.dma("gpsimd", s_ex, mx_io[k][0].ap(), mxs[0:1, 0:2])
                    P.wait("gpsimd", s_ex)
                    P.op("gpsimd", lambda e, k=k: e.collective_compute(
                        "AllReduce", ALU.max, replica_groups=[list(range(world))],
                        ins=[mx_io[k][0].ap().opt()], outs=[mx_io[k][1].ap().opt()]), s_cc)
                    P.wait("sync", s_cc)
                    P.dma("sync", s_hl, mxs[0:1, 0:2], mx_io[k][1].ap())
                    P.wait("vector", s_hl)

                def mk_scales(src_col, dst_cols):
                    P.op("vector", lambda e, src_col=src_col: e.tensor_scalar(
                        out=mxs[0:1, 8:9], in0=mxs[0:1, src_col:src_col + 1],
                        scalar1=1.0 / 32767.0, scalar2=1e-12, op0=ALU.mult, op1=ALU.max))
                    P.op("vector", lambda e: e.reciprocal(
                        out=mxs[0:1, 9:10], in_=mxs[0:1, 8:9]), s_ve)
                    P.wait("tensor", s_ve)
                    for sc, dc in ((8, dst_cols[0]), (9, dst_cols[1])):
                        P.op("tensor", lambda e, sc=sc: e.matmul(
                            psA[:, 0:1], ones_f[0:1, :], mxs[0:1, sc:sc + 1],
                            start=True, stop=True), s_mm)
                        P.wait("vector", s_mm)
                        P.op("vector", lambda e, dc=dc: e.tensor_copy(
                            out=_pp(scl, dc), in_=psA[:, 0:1]), s_ve)
                        P.wait("tensor", s_ve)

                with ExitStack() as yx:
                    YE = yx.enter_context
                    ybuf = YE(nc.sbuf_tensor("f1_y", [128, NC * TOK], F32))
                    fused_ln(False, ybuf, TOK, psA, psB, scr)
                    P.wait("vector", s_ve)
                    for c in range(NC):
                        P.op("vector", lambda e, c=c: e.tensor_reduce(
                            out=_pp(scr, c), in_=ybuf[:, c * TOK:(c + 1) * TOK], axis=AX.X,
                            op=ALU.max, apply_absolute_value=True))
                    P.op("vector", lambda e: e.tensor_reduce(
                        out=_pp(scr, 8), in_=scr[:, 0:8], axis=AX.X, op=ALU.max,
                        apply_absolute_value=True), s_ve)
                    cross_part_max(_pp(scr, 8), 0)
                    P.op("vector", lambda e: e.tensor_copy(out=mxs[0:1, 1:2], in_=mxs[0:1, 0:1]), s_ve)
                    allreduce_max(0)
                    mk_scales(0, (0, 1))
                    for c in range(NC):
                        for tt in range(NT):
                            ysl = ybuf[:, c * TOK + tt * 512: c * TOK + (tt + 1) * 512]
                            hsl = h_bf[:, c * TOK + tt * 512: c * TOK + (tt + 1) * 512]
                            P.op("vector", lambda e, ysl=ysl: e.tensor_scalar(
                                out=scr[:, 512:1024], in0=ysl, scalar1=_pp(scl, 1),
                                scalar2=MAGIC, op0=ALU.mult, op1=ALU.add))
                            P.op("vector", lambda e, hsl=hsl: e.tensor_scalar(
                                out=hsl, in0=scr[:, 512:1024], scalar1=-MAGIC,
                                scalar2=_pp(scl, 0), op0=ALU.add, op1=ALU.mult), s_ve)

                phase_barrier()
                macc = FE(nc.sbuf_tensor("f1_macc", [128, 4], F32))
                tla = FE(nc.sbuf_tensor("f1_tla", [128, 2 * 512], F32))
                p12x = ExitStack()
                tlb = p12x.enter_context(nc.sbuf_tensor("f1_tlb", [128, 2 * 512], F32))
                tlc = p12x.enter_context(nc.sbuf_tensor("f1_tlc", [128, 2 * 512], F32))
                ugx = ExitStack()
                UE = ugx.enter_context
                qwb = UE(nc.sbuf_tensor("f1_qwb", [128, 2 * 8192], BF16))
                stg = UE(nc.sbuf_tensor("f1_stg", [128, 4 * 512], F32))
                P.op("vector", lambda e: e.memset(macc[:, :], 0.0), s_ms)
                P.wait("vector", s_ms)
                up_io = wq_up_d.ap().rearrange("c p n -> p c n")
                gt_io = wq_gate_d.ap().rearrange("c p n -> p c n")
                mm_marks = {}
                ex_marks = {}
                for tt in range(NT):
                    P.wait("sync", s_mm)
                    for mg in range(6):
                        r = mg % 2
                        key = (tt, mg)
                        pkey = (tt, mg - 2) if mg >= 2 else ((tt - 1, 4 + mg) if tt >= 1 else None)
                        if pkey is not None and pkey in mm_marks:
                            P.wait("sync", s_mm, mm_marks[pkey])
                        P.dma("sync", s_q[r], qwb[:, r * 8192:r * 8192 + 4096],
                              up_io[:, :, mg * 512:(mg + 1) * 512])
                        qlm = P.dma("sync", s_q[r], qwb[:, r * 8192 + 4096:(r + 1) * 8192],
                                    gt_io[:, :, mg * 512:(mg + 1) * 512], fresh=False)
                        P.wait("tensor", s_q[r], qlm)
                        for mi in range(4):
                            m = mg * 4 + mi
                            for which in range(2):
                                base = which * 4096
                                dram = up_raw_d if which == 0 else gate_raw_d
                                mcol = which
                                ps = psA if which == 0 else psB
                                P.wait("tensor", s_ve)
                                for c in range(NC):
                                    mm_marks[key] = P.op("tensor", lambda e, r=r, base=base, c=c, mi=mi, ps=ps, tt=tt: e.matmul(
                                        ps[:, :],
                                        qwb[:, r * 8192 + base + c * 512 + mi * 128:
                                            r * 8192 + base + c * 512 + (mi + 1) * 128],
                                        h_bf[:, c * TOK + tt * 512: c * TOK + (tt + 1) * 512],
                                        start=(c == 0), stop=(c == NC - 1)), s_mm if c == NC - 1 else None)
                                P.wait("vector", s_mm)
                                skey = (m % 2) * 2 + which
                                ekey = (tt * 24 + m - 2, which)
                                if ekey in ex_marks:
                                    P.wait("vector", s_e[(m % 2) * 2 + which], ex_marks[ekey])
                                P.op("vector", lambda e, skey=skey, ps=ps: e.tensor_copy(
                                    out=stg[:, skey * 512:(skey + 1) * 512], in_=ps[:, :]))
                                P.op("vector", lambda e, ps=ps: e.tensor_reduce(
                                    out=_pp(scr, 15), in_=ps[:, :], axis=AX.X, op=ALU.max,
                                    apply_absolute_value=True))
                                P.op("vector", lambda e, mcol=mcol: e.tensor_max(
                                    out=_pp(macc, mcol), in0=_pp(macc, mcol), in1=_pp(scr, 15)), s_ve)
                                P.wait("gpsimd", s_ve)
                                ex_marks[(tt * 24 + m, which)] = P.dma(
                                    "gpsimd", s_e[skey], dram.ap()[m][:, tt * 512:(tt + 1) * 512],
                                    stg[:, skey * 512:(skey + 1) * 512])

                for q_ in range(4):
                    P.wait("vector", s_e[q_])
                    P.wait("gpsimd", s_e[q_])
                ugx.close()
                phase_barrier()
                cross_part_max(_pp(macc, 0), 0)
                cross_part_max(_pp(macc, 1), 1)
                allreduce_max(1)
                mk_scales(0, (2, 3))
                mk_scales(1, (4, 5))

                def qdq2(dst, src, rs_col, s_col, tmp, sem=None):
                    P.op("vector", lambda e, tmp=tmp, src=src, rs_col=rs_col: e.tensor_scalar(
                        out=tmp, in0=src, scalar1=_pp(scl, rs_col), scalar2=MAGIC,
                        op0=ALU.mult, op1=ALU.add))
                    return P.op("vector", lambda e, dst=dst, tmp=tmp, s_col=s_col: e.tensor_scalar(
                        out=dst, in0=tmp, scalar1=-MAGIC, scalar2=_pp(scl, s_col),
                        op0=ALU.add, op1=ALU.mult), sem)

                P.op("vector", lambda e: e.memset(macc[:, 2:4], 0.0), s_ms)
                P.wait("vector", s_ms)
                ex_marks = {}
                ve_marks = {}
                for tt in range(NT):
                    for m in range(24):
                        r = m % 2
                        key = (tt, m)
                        pkey = (tt, m - 2) if m >= 2 else ((tt - 1, 22 + m) if tt >= 1 else None)
                        if pkey is not None and pkey in ve_marks:
                            P.wait("sync", s_ve, ve_marks[pkey])
                        mk = P.dma("sync", s_l[r], tla[:, r * 512:(r + 1) * 512],
                                   gate_raw_d.ap()[m][:, tt * 512:(tt + 1) * 512])
                        P.wait("vector", s_l[r], mk)
                        if pkey is not None and pkey in ex_marks:
                            P.wait("vector", s_e[r], ex_marks[pkey])
                        qdq2(tlb[:, r * 512:(r + 1) * 512], tla[:, r * 512:(r + 1) * 512],
                             5, 4, scr[:, 512:1024], sem=s_ve)
                        P.wait("scalar", s_ve)
                        P.op("scalar", lambda e, r=r: e.activation(
                            out=tlc[:, r * 512:(r + 1) * 512], in_=tlb[:, r * 512:(r + 1) * 512],
                            func=AF.Sigmoid), s_sc)
                        P.wait("vector", s_sc)
                        P.op("vector", lambda e, r=r: e.tensor_scalar(
                            out=tlc[:, r * 512:(r + 1) * 512], in0=tlc[:, r * 512:(r + 1) * 512],
                            scalar1=65536.0, scalar2=MAGIC, op0=ALU.mult, op1=ALU.add))
                        P.op("vector", lambda e, r=r: e.tensor_scalar(
                            out=tlc[:, r * 512:(r + 1) * 512], in0=tlc[:, r * 512:(r + 1) * 512],
                            scalar1=-MAGIC, scalar2=1.0 / 65536.0, op0=ALU.add, op1=ALU.mult))
                        P.op("vector", lambda e, r=r: e.tensor_mul(
                            out=tlb[:, r * 512:(r + 1) * 512], in0=tlb[:, r * 512:(r + 1) * 512],
                            in1=tlc[:, r * 512:(r + 1) * 512]))
                        P.op("vector", lambda e, r=r: e.tensor_reduce(
                            out=_pp(scr, 15), in_=tlb[:, r * 512:(r + 1) * 512], axis=AX.X,
                            op=ALU.max, apply_absolute_value=True))
                        ve_marks[key] = P.op("vector", lambda e: e.tensor_max(
                            out=_pp(macc, 2), in0=_pp(macc, 2), in1=_pp(scr, 15)), s_ve)
                        P.wait("gpsimd", s_ve)
                        ex_marks[key] = P.dma("gpsimd", s_e[r],
                                              t2_d.ap()[m][:, tt * 512:(tt + 1) * 512],
                                              tlb[:, r * 512:(r + 1) * 512])
                P.wait("vector", s_e[0])
                P.wait("vector", s_e[1])
                P.wait("gpsimd", s_e[0])
                P.wait("gpsimd", s_e[1])
                cross_part_max(_pp(macc, 2), 0)
                P.op("vector", lambda e: e.tensor_copy(out=mxs[0:1, 1:2], in_=mxs[0:1, 0:1]), s_ve)
                allreduce_max(2)
                mk_scales(0, (6, 7))

                ex_marks = {}
                ve_marks = {}
                for tt in range(NT):
                    for m in range(24):
                        r = m % 2
                        key = (tt, m)
                        pkey = (tt, m - 2) if m >= 2 else ((tt - 1, 22 + m) if tt >= 1 else None)
                        if pkey is not None and pkey in ve_marks:
                            P.wait("sync", s_ve, ve_marks[pkey])
                        P.dma("sync", s_l[r], tla[:, r * 512:(r + 1) * 512],
                              t2_d.ap()[m][:, tt * 512:(tt + 1) * 512])
                        mk = P.dma("sync", s_l[r], tlc[:, r * 512:(r + 1) * 512],
                                   up_raw_d.ap()[m][:, tt * 512:(tt + 1) * 512], fresh=False)
                        P.wait("vector", s_l[r], mk)
                        if pkey is not None and pkey in ex_marks:
                            P.wait("vector", s_e[r], ex_marks[pkey])
                        qdq2(tlb[:, r * 512:(r + 1) * 512], tla[:, r * 512:(r + 1) * 512],
                             7, 6, scr[:, 512:1024])
                        qdq2(tlc[:, r * 512:(r + 1) * 512], tlc[:, r * 512:(r + 1) * 512],
                             3, 2, scr[:, 1024:1536])
                        P.op("vector", lambda e, r=r: e.tensor_mul(
                            out=tlb[:, r * 512:(r + 1) * 512], in0=tlb[:, r * 512:(r + 1) * 512],
                            in1=tlc[:, r * 512:(r + 1) * 512]))
                        P.op("vector", lambda e, r=r: e.tensor_reduce(
                            out=_pp(scr, 15), in_=tlb[:, r * 512:(r + 1) * 512], axis=AX.X,
                            op=ALU.max, apply_absolute_value=True))
                        ve_marks[key] = P.op("vector", lambda e: e.tensor_max(
                            out=_pp(macc, 3), in0=_pp(macc, 3), in1=_pp(scr, 15)), s_ve)
                        P.wait("gpsimd", s_ve)
                        ex_marks[key] = P.dma("gpsimd", s_e[r],
                                              t3_d.ap()[m][:, tt * 512:(tt + 1) * 512],
                                              tlb[:, r * 512:(r + 1) * 512])
                P.wait("vector", s_e[0])
                P.wait("vector", s_e[1])
                P.wait("gpsimd", s_e[0])
                P.wait("gpsimd", s_e[1])
                cross_part_max(_pp(macc, 3), 0)
                P.op("vector", lambda e: e.tensor_copy(out=mxs[0:1, 1:2], in_=mxs[0:1, 0:1]), s_ve)
                allreduce_max(3)
                mk_scales(0, (8, 9))

                p12x.close()
                phase_barrier()
                obuf = FE(nc.sbuf_tensor("f1_obuf", [128, 24 * 512], BF16))
                dwb = FE(nc.sbuf_tensor("f1_dwb", [128, 24 * 512], BF16))
                dn_io = wq_down_d.ap().rearrange("c p n -> p c n")
                for tt in range(NT):
                    ve_marks = {}
                    for m in range(24):
                        r = m % 2
                        if m >= 2:
                            P.wait("sync", s_ve, ve_marks[m - 2])
                        mk = P.dma("sync", s_hl, tla[:, r * 512:(r + 1) * 512],
                                   t3_d.ap()[m][:, tt * 512:(tt + 1) * 512])
                        P.wait("vector", s_hl, mk)
                        ve_marks[m] = qdq2(obuf[:, m * 512:(m + 1) * 512],
                                           tla[:, r * 512:(r + 1) * 512],
                                           9, 8, scr[:, 512:1024], sem=s_ve)
                    for ocg in range(2):
                        P.wait("sync", s_mm)
                        for c in range(24):
                            P.dma("sync", s_hl, dwb[:, c * 512:(c + 1) * 512],
                                  dn_io[:, c, ocg * 512:(ocg + 1) * 512], fresh=(c == 0))
                        P.wait("tensor", s_hl)
                        P.wait("tensor", s_ve)
                        for oci in range(4):
                            oc = ocg * 4 + oci
                            ps = psA if oci % 2 == 0 else psB
                            P.wait("tensor", s_ve)
                            for c in range(24):
                                P.op("tensor", lambda e, c=c, oci=oci, ps=ps: e.matmul(
                                    ps[:, :], dwb[:, c * 512 + oci * 128: c * 512 + (oci + 1) * 128],
                                    obuf[:, c * 512:(c + 1) * 512],
                                    start=(c == 0), stop=(c == 23)), s_mm if c == 23 else None)
                            P.wait("vector", s_mm)
                            xsl = x_fm[:, oc * TOK + tt * 512: oc * TOK + (tt + 1) * 512]
                            P.op("vector", lambda e, ps=ps, xsl=xsl: e.tensor_add(
                                out=xsl, in0=ps[:, :], in1=xsl), s_ve)

        attention(0)
        if n_stages >= 2:
            ffn0()
        if n_stages >= 3:
            attention(1)
        if n_stages >= 4:
            ffn1()

        P.wait("sync", s_ve)
        P.wait("sync", s_sc)
        P.wait("sync", s_mm)
        for c in range(NC):
            P.dma("sync", s_st, out_d.ap()[c * 128:(c + 1) * 128, :],
                  x_fm[:, c * TOK:(c + 1) * TOK], fresh=(c == 0))
        P.wait("sync", s_st)

        block = E(nc.Block())
        P.run(block)

    return nc



def _bf(x):
    return np.asarray(x, dtype=np.float32).astype(ml_dtypes.bfloat16)


def _tile_io(wT, m_tiles, c_tiles):
    t = wT.reshape(c_tiles, 128, m_tiles, 128)
    return np.ascontiguousarray(t.transpose(2, 0, 1, 3))


def prep_inputs(inputs):
    f32 = lambda a: np.ascontiguousarray(np.asarray(a, dtype=np.float32))
    x = f32(inputs["L0_attn_x"])

    sh = {}
    sh["l0_wdq"] = f32(np.asarray(inputs["L0_attn_w_down_q"]).T)
    sh["l0_wdkv"] = f32(np.asarray(inputs["L0_attn_w_down_kv"]).T)
    sh["l0_wuq"] = f32(np.asarray(inputs["L0_attn_w_up_q"]).T)
    sh["l0_wuk"] = f32(np.asarray(inputs["L0_attn_w_up_k"]).T)
    sh["l0_wuv"] = f32(np.asarray(inputs["L0_attn_w_up_v"]).T)
    sh["l0_wo"] = f32(np.asarray(inputs["L0_attn_w_out"]).T)
    lnp0 = np.zeros((128, 16), np.float32)
    lnp0[:, 0:8] = f32(inputs["L0_attn_ln_scale"]).reshape(8, 128).T
    lnp0[:, 8:16] = f32(inputs["L0_attn_ln_bias"]).reshape(8, 128).T
    sh["l0_lnp"] = lnp0
    sh["l0_w1t"] = _tile_io(f32(np.asarray(inputs["L0_ffn_w1_w"]).T), 32, 8)
    sh["l0_w2t"] = _tile_io(f32(np.asarray(inputs["L0_ffn_w2_w"]).T), 32, 8)
    sh["l0_h1t"] = _tile_io(f32(np.asarray(inputs["L0_ffn_h1_w"]).T), 32, 32)
    sh["l0_h2t"] = _tile_io(f32(np.asarray(inputs["L0_ffn_h2_w"]).T), 32, 32)
    sh["l0_w3t"] = _tile_io(f32(np.asarray(inputs["L0_ffn_w3_w"]).T), 8, 32)
    fsv = np.zeros((128, 136), np.float32)
    fsv[:, 0:32] = f32(inputs["L0_ffn_w1_s"]).reshape(32, 128).T
    fsv[:, 32:64] = f32(inputs["L0_ffn_w2_s"]).reshape(32, 128).T
    fsv[:, 64:96] = f32(inputs["L0_ffn_h1_s"]).reshape(32, 128).T
    fsv[:, 96:128] = f32(inputs["L0_ffn_h2_s"]).reshape(32, 128).T
    fsv[:, 128:136] = f32(inputs["L0_ffn_w3_s"]).reshape(8, 128).T
    sh["l0_fs"] = fsv

    wq2 = f32(np.asarray(inputs["L1_attn_W_q"]).transpose(1, 0, 2).reshape(1024, 1024))
    wk2 = f32(np.asarray(inputs["L1_attn_W_k"]).transpose(1, 0, 2).reshape(1024, 1024))
    wv2 = f32(np.asarray(inputs["L1_attn_W_v"]).transpose(1, 0, 2).reshape(1024, 1024))
    sh["l1_wqt"] = _tile_io(wq2, 8, 8)
    sh["l1_wkt"] = _tile_io(wk2, 8, 8)
    sh["l1_wv"] = wv2
    sh["l1_wo"] = f32(inputs["L1_attn_W_o"])
    b1 = np.zeros((128, 16), np.float32)
    b1[:, 0:8] = f32(inputs["L1_attn_b_q"]).reshape(-1).reshape(8, 128).T
    b1[:, 8:16] = f32(inputs["L1_attn_b_k"]).reshape(-1).reshape(8, 128).T
    sh["l1_b"] = b1
    sh["l1_bv_row"] = f32(inputs["L1_attn_b_v"]).reshape(1, 1024)
    sh["l1_bo_row"] = f32(inputs["L1_attn_b_o"]).reshape(1, 1024)
    lnp1 = np.zeros((128, 16), np.float32)
    lnp1[:, 0:8] = f32(inputs["L1_attn_ln_w"]).reshape(8, 128).T
    lnp1[:, 8:16] = f32(inputs["L1_attn_ln_b"]).reshape(8, 128).T
    sh["l1_lnp"] = lnp1
    sh["l1_wup_oi"] = f32(inputs["L1_ffn_w_up"])
    sh["l1_wgate_oi"] = f32(inputs["L1_ffn_w_gate"])
    sh["l1_wdown_oi"] = f32(inputs["L1_ffn_w_down"])
    sh["l1_wup_io"] = f32(np.asarray(inputs["L1_ffn_w_up"]).T)
    sh["l1_wgate_io"] = f32(np.asarray(inputs["L1_ffn_w_gate"]).T)
    sh["l1_wdown_io"] = f32(np.asarray(inputs["L1_ffn_w_down"]).T)

    sh["ones_f"] = np.ones((128, 128), np.float32)
    sh["ident"] = np.eye(128, dtype=np.float32)
    sh["ones512"] = np.ones((1, 512), np.float32)
    sel = np.zeros((2, 256), np.float32)
    sel[0, 0:128] = 1.0
    sel[1, 128:256] = 1.0
    sh["sel2"] = sel
    rep = np.zeros((4, 128), np.float32)
    for r in range(4):
        rep[r, r * 32:(r + 1) * 32] = 1.0
    sh["rep4"] = rep
    perm = np.zeros((128, 128), np.float32)
    for i in range(64):
        perm[2 * i, 2 * i + 1] = 1.0
        perm[2 * i + 1, 2 * i] = 1.0
    sh["permT"] = _bf(perm)
    cstv = np.zeros((128, 8), np.float32)
    cstv[:, 0] = 1e-5
    cstv[:, 1] = 1.0 / 1024.0
    cstv[:, 2] = 0.125
    cstv[:, 3] = -1.0
    sh["cst"] = cstv

    inv_freq = 1.0 / (10000.0 ** (np.arange(0, 64, 2, dtype=np.float32) / 64))
    t_all = np.arange(S, dtype=np.float32)
    freqs = np.outer(t_all, inv_freq)
    cos_t = np.concatenate([np.cos(freqs), np.cos(freqs)], -1)[:, ::2]
    sin_t = np.concatenate([np.sin(freqs), np.sin(freqs)], -1)[:, ::2]

    in_maps = []
    for core in range(8):
        b, g = core // 2, core % 2
        im = dict(sh)
        xb = x[b]
        own = xb.reshape(32, 128, D)[g::2].reshape(TOK, D)
        im["x_in"] = np.ascontiguousarray(own.T)
        pos = (np.arange(TOK) // 128 * 2 + g) * 128 + np.arange(TOK) % 128
        C = np.zeros((128, TOK), np.float32)
        Sn = np.zeros((128, TOK), np.float32)
        for i in range(32):
            C[2 * i, :] = cos_t[pos, i]; C[2 * i + 1, :] = cos_t[pos, i]
            Sn[2 * i, :] = -sin_t[pos, i]; Sn[2 * i + 1, :] = sin_t[pos, i]
        C[64:128] = C[0:64]; Sn[64:128] = Sn[0:64]
        im["ropeC"] = _bf(C)
        im["ropeS"] = _bf(Sn)
        masks = np.zeros((NT, 8, 128, 512), np.float32)
        for qg in range(NT):
            qloc = qg * 512 + np.arange(512)
            qpos = (qloc // 128 * 2 + g) * 128 + qloc % 128
            for jj in range(8):
                j = 8 * qg + jj
                kpos = j * 128 + np.arange(128)
                masks[qg, jj] = (kpos[:, None] <= qpos[None, :]).astype(np.float32)
        im["l1_masks"] = _bf(masks)
        in_maps.append(im)
    return in_maps


def assemble(results):
    out = np.zeros((B, S, D), np.float32)
    for core in range(8):
        b, g = core // 2, core % 2
        xo = np.asarray(results[core]["out_x"])
        out[b].reshape(32, 128, D)[g::2] = xo.T.reshape(16, 128, D)
    return out


_NC_CACHE = {}


def kernel(**inputs):
    n = N_STAGES
    if n not in _NC_CACHE:
        _NC_CACHE[n] = build_nc(n)
    nc = _NC_CACHE[n]
    in_maps = prep_inputs(inputs)
    res = run_bass_kernel_spmd(nc, in_maps, core_ids=list(range(8)))
    return assemble(res.results)
